# version 09
# baseline (speedup 1.0000x reference)
import numpy as np
from contextlib import ExitStack
from collections import defaultdict

import concourse.bass as bass
import concourse.mybir as mybir
from concourse.bass_utils import run_bass_kernel_spmd

F32 = mybir.dt.float32
BF16 = mybir.dt.bfloat16
I32 = mybir.dt.int32
AF = mybir.ActivationFunctionType
ALU = mybir.AluOpType
AX = mybir.AxisListType

P = 128
D = 1024
NH = 16
NE = 8
FH = 4096
NCORES = 8


class MockInst:
    def then_inc(self, *a, **k):
        return self


class K:

    def __init__(self, dry, sems=None, chk=None):
        self.dry = dry
        self.sems = sems or {}
        self.cnt = defaultdict(int)
        self.chk = chk if chk is not None else {}

    def inc(self, inst, sem, n):
        self.cnt[sem] += n
        if not isinstance(inst, MockInst):
            inst.then_inc(self.sems[sem], n)

    def ck(self, key, sem):
        v = (sem, self.cnt[sem])
        if self.dry:
            assert key not in self.chk, f"dup checkpoint {key}"
            self.chk[key] = v
        else:
            assert self.chk[key] == v, (key, self.chk[key], v)

    def wait(self, eng, key):
        assert key in self.chk, f"forward wait on {key}"
        if eng.real is None:
            return
        sem, val = self.chk[key]
        if val > 0:
            eng.real.wait_ge(self.sems[sem], val)


class E:

    def __init__(self, k, real, sem, chain):
        self.k = k
        self.real = real
        self.sem = sem
        self.chain = chain

    def __getattr__(self, name):
        def f(*a, inc=None, **kw):
            k = self.k
            if self.chain:
                prev = k.cnt[self.sem]
                if prev > 0 and self.real is not None:
                    self.real.wait_ge(k.sems[self.sem], prev)
            inst = (getattr(self.real, name)(*a, **kw)
                    if self.real is not None else MockInst())
            n = inc if inc is not None else (1 if self.chain else 0)
            if n:
                k.inc(inst, self.sem, n)
            return inst
        return f

    def dma(self, out, in_, sem):
        k = self.k
        inst = (self.real.dma_start(out=out, in_=in_)
                if self.real is not None else MockInst())
        k.inc(inst, sem, 16)
        return inst


def build_program(S=2048, NSEQ=2, stop_after=None):
    TT = S // 128
    KT = TT
    QW = min(512, S)
    NQH = S // QW
    NT5 = S // 512
    MW = min(1024, S // 2)
    NM = S // MW
    MTC = MW // 128
    NPAIR = NH // 2
    NG1 = NM * NE * 16
    NZ2 = NM * NE * 4
    g_list = [(ei, gi) for m in range(NM) for ei in range(NE) for gi in range(16)]
    z_list = [(ei, qi) for m in range(NM) for ei in range(NE) for qi in range(4)]

    nc = bass.Bass("TRN2", target_bir_lowering=False)

    x_in = nc.dram_tensor("x", [NSEQ * P, TT * D], F32, kind="ExternalInput").ap()
    wq_in = nc.dram_tensor("wq", [P, 8192], F32, kind="ExternalInput").ap()
    wk_in = nc.dram_tensor("wk", [P, 8192], F32, kind="ExternalInput").ap()
    wv_in = nc.dram_tensor("wv", [P, 8192], F32, kind="ExternalInput").ap()
    wf_in = nc.dram_tensor("wf", [P, 8192], F32, kind="ExternalInput").ap()
    bff_in = nc.dram_tensor("bff", [1, D], F32, kind="ExternalInput").ap()
    rw_in = nc.dram_tensor("rw", [P, 64], F32, kind="ExternalInput").ap()
    rb_in = nc.dram_tensor("rb", [1, NE], F32, kind="ExternalInput").ap()
    w1_in = nc.dram_tensor("w1", [NE * 16 * P, 8 * 512], F32, kind="ExternalInput").ap()
    w2_in = nc.dram_tensor("w2", [NE * 4 * P, 32 * 256], F32, kind="ExternalInput").ap()
    out_d = nc.dram_tensor("out", [NSEQ * P, TT * D], F32, kind="ExternalOutput").ap()
    W_INS = [wq_in, wk_in, wv_in, wf_in]

    with ExitStack() as top:
        e = top.enter_context

        resid = e(nc.sbuf_tensor("resid", [P, TT, D], BF16))
        harena = e(nc.sbuf_tensor("harena", [P, 8, S], BF16))
        qkarena = e(nc.sbuf_tensor("qkarena", [P, 16, S], BF16))
        vext = e(nc.sbuf_tensor("vext", [P, TT, NH, 65], BF16))
        wbuf = e(nc.sbuf_tensor("wbuf", [P, 2, 8, 512], BF16))
        abuf = e(nc.sbuf_tensor("abuf", [P, 2, 2, 2 * QW], BF16))
        hslot = e(nc.sbuf_tensor("hslot", [P, 2, D], BF16))
        rbs = e(nc.sbuf_tensor("rbs", [64, QW], F32))
        rrowb = e(nc.sbuf_tensor("rrowb", [1, QW], BF16))
        xln = e(nc.sbuf_tensor("xln", [P, D], F32))
        xlnT = e(nc.sbuf_tensor("xlnT", [P, D], F32))
        stats = e(nc.sbuf_tensor("stats", [P, 24], F32))
        msk = e(nc.sbuf_tensor("msk", [P, NE], F32))
        elb = e(nc.sbuf_tensor("elb", [P, NE], F32))
        sctab = e(nc.sbuf_tensor("sctab", [P, TT, NE], F32))
        rwv = e(nc.sbuf_tensor("rwv", [P, 8, NE], F32))
        rbr = e(nc.sbuf_tensor("rbr", [1, NE], F32))
        bffr = e(nc.sbuf_tensor("bffr", [1, D], BF16))
        identb = e(nc.sbuf_tensor("identb", [P, P], BF16))
        identf = e(nc.sbuf_tensor("identf", [P, P], F32))
        ones_b1 = e(nc.sbuf_tensor("ones_b1", [1, P], BF16))
        ones_f1 = e(nc.sbuf_tensor("ones_f1", [1, P], F32))
        negM = e(nc.sbuf_tensor("negM", [P, 1], F32))
        epsb = e(nc.sbuf_tensor("epsb", [P, 1], F32))
        qmc = e(nc.sbuf_tensor("qmc", [P, 2], BF16))
        mrow = e(nc.sbuf_tensor("mrow", [1, 8], F32))
        mrowb = e(nc.sbuf_tensor("mrowb", [1, 1], BF16))
        iota1 = e(nc.sbuf_tensor("iota1", [P, P], I32))
        iota2 = e(nc.sbuf_tensor("iota2", [P, P], I32))

        residv = resid[:, :, :]
        hT = harena[:, :, :]
        oT = hT
        x2T = hT
        qT = qkarena[:, 0:8, :]
        kT = qkarena[:, 8:16, :]
        hh = qkarena[:, :, :].rearrange("p c s -> p (c s)")[:, 0:32 * MW].rearrange(
            "p (c m) -> p c m", m=MW)
        if TT * NH * 65 >= 16384:
            w2sl = [vext[:, :, :, :].rearrange("p a b c -> p (a b c)")[:, kk * 8192:(kk + 1) * 8192]
                    .rearrange("p (c f) -> p c f", f=256) for kk in range(2)]
        else:
            w2b = e(nc.sbuf_tensor("w2b", [P, 2, 32, 256], BF16))
            w2sl = [w2b[:, 0, :, :], w2b[:, 1, :, :]]
        wsl = [wbuf[:, 0, :, :], wbuf[:, 1, :, :]]

        SC = {n: i for i, n in enumerate(
            ["mean0", "mean1", "ss0", "ss1", "msq0", "msq1", "var0", "var1",
             "lnv0", "lnv1", "rstd0", "rstd1", "negmz0", "negmz1",
             "maxl0", "maxl1", "negml0", "negml1", "sumr0", "sumr1",
             "mp0", "mp1"])}

        def st(name, par=None):
            i = SC[name if par is None else name + str(par % 2)]
            return stats[:, i:i + 1]

        def pgroup(*specs):
            hs = []
            with ExitStack() as stx:
                for nm, shp, dt in specs:
                    hs.append(stx.enter_context(nc.psum_tensor(nm, shp, dt)))
            return hs

        psT0, psT1, pdum1, pw0, pw1 = pgroup(
            ("psT0", [P, D], BF16), ("psT1", [P, D], BF16), ("pdum1", [P, D], F32),
            ("pw0", [P, 512], F32), ("pw1", [P, 512], F32))
        pv0, pv1 = pgroup(("pv0", [P, D], F32), ("pv1", [P, D], F32))
        pA_, pB_, pOA0, pOA1, pOB0, pOB1 = pgroup(
            ("pA_", [P, 2 * QW], F32), ("pB_", [P, 2 * QW], F32),
            ("pOA0", [P, QW], F32), ("pOA1", [P, QW], F32),
            ("pOB0", [P, QW], F32), ("pOB1", [P, QW], F32))
        pOA = [pOA0, pOA1]
        pOB = [pOB0, pOB1]
        pAO0, pAO1 = pgroup(("pAO0", [P, D], F32), ("pAO1", [P, D], F32))
        pxT0, pxT1, pL0, pL1, pdum2 = pgroup(
            ("pxT0", [P, D], F32), ("pxT1", [P, D], F32),
            ("pL0", [P, NE], F32), ("pL1", [P, NE], F32), ("pdum2", [P, D], F32))
        pa, pg_, py0, py1 = pgroup(("pa", [P, MW], F32), ("pg", [P, MW], F32),
                                   ("py0", [P, 512], F32), ("py1", [P, 512], F32))
        psT = [psT0, psT1]
        pw = [pw0, pw1]
        pv = [pv0, pv1]
        pAO = [pAO0, pAO1]
        pxT = [pxT0, pxT1]
        pL = [pL0, pL1]
        py = [py0, py1]

        sem_names = ["dx", "dwa", "dwb", "dza", "dzb", "dsm", "dsb", "do",
                     "pe", "dv", "sc", "gp", "sp"]
        sems = {n: e(nc.semaphore(n)) for n in sem_names}

        block = e(nc.Block())

        def emit(k, t, v, s_, g, sy):
            g.iota(iota1[:], pattern=[[1, P]], base=0, channel_multiplier=0)
            g.iota(iota2[:], pattern=[[0, P]], base=0, channel_multiplier=1)
            g.memset(ones_b1[:], 1.0)
            g.memset(ones_f1[:], 1.0)
            g.memset(epsb[:], 1e-5)
            k.ck("init_gp", "gp")
            k.wait(v, "init_gp")
            v.tensor_tensor(out=identb[:], in0=iota1[:], in1=iota2[:], op=ALU.is_equal)
            v.tensor_copy(out=identf[:], in_=identb[:])
            k.ck("ident", "dv")
            sy.dma(rwv[:, :, :], rw_in[:, :].rearrange("p (c e) -> p c e", e=NE), "dsm")
            sy.dma(rbr[:, :], rb_in[:, :], "dsm")
            g.dma(bffr[:, :], bff_in[:, :], "dsb")
            k.ck("smload", "dsm")
            k.ck("smload_b", "dsb")

            for s in range(NSEQ):
                emit_seq(k, t, v, s_, g, sy, s)

            if g.real is not None:
                g.real.wait_ge(k.sems["do"], 16 * NSEQ * NM)

        def emit_seq(k, t, v, s_, g, sy, s):
            HTT = TT // 2
            for hx in range(2):
                if s > 0:
                    k.wait(g, ("outd", s - 1, hx))
                g.dma(residv[:, hx * HTT:(hx + 1) * HTT, :],
                      x_in[s * P:(s + 1) * P, :].rearrange("p (t d) -> p t d", d=D)
                      [:, hx * HTT:(hx + 1) * HTT, :], "dx")
                k.ck(("xld", s, hx), "dx")

            def attn_wload(wi, h, wkey):
                dws = "dwa" if h == 0 else "dwb"
                k.wait(g, wkey)
                g.dma(wsl[h],
                      W_INS[wi][:, :].rearrange("p (h c f) -> p h c f", h=2, f=512)[:, h],
                      dws)
                k.ck(("wld", s, wi, h), dws)

            for h in range(2):
                attn_wload(0, h, ("ffn_done", s - 1) if s > 0 else "init_gp")
            if s > 0:
                k.wait(g, ("ffn_done", s - 1))
            g.memset(vext[:, :, :, 64:65], 1.0)
            k.ck(("vones", s), "gp")

            for tt in range(TT):
                xt = residv[:, tt, :]
                if tt % HTT == 0:
                    k.wait(s_, ("xld", s, tt // HTT))
                    if tt == 0 and s > 0:
                        k.wait(s_, ("ffn_done", s - 1))
                s_.activation(out=pdum1[:, :], in_=xt, func=AF.Copy, scale=1.0 / D,
                              accum_out=st("mean", tt))
                s_.activation(out=pdum1[:, :], in_=xt, func=AF.Square, scale=1.0 / 32.0,
                              accum_out=st("ss", tt))
                k.ck(("l1st", s, tt), "sc")
                if tt % HTT == 0:
                    k.wait(v, ("xld", s, tt // HTT))
                    if tt == 0 and s > 0:
                        k.wait(v, ("ffn_done", s - 1))
                k.wait(v, ("l1st", s, tt))
                v.tensor_scalar(out=st("msq", tt), in0=st("mean", tt),
                                scalar1=st("mean", tt), scalar2=None, op0=ALU.mult)
                v.tensor_tensor(out=st("var", tt), in0=st("ss", tt), in1=st("msq", tt),
                                op=ALU.subtract)
                k.ck(("l1var", s, tt), "dv")
                k.wait(s_, ("l1var", s, tt))
                s_.activation(out=st("lnv", tt), in_=st("var", tt), func=AF.Ln,
                              bias=epsb[:], scale=1.0)
                s_.activation(out=st("rstd", tt), in_=st("lnv", tt), func=AF.Exp, scale=-0.5)
                k.ck(("l1rstd", s, tt), "sc")
                k.wait(v, ("l1rstd", s, tt))
                v.tensor_scalar(out=st("negmz", tt), in0=st("mean", tt),
                                scalar1=st("rstd", tt), scalar2=-1.0,
                                op0=ALU.mult, op1=ALU.mult)
                if tt >= 2:
                    k.wait(v, ("l1tr", s, tt - 2))
                v.tensor_scalar(out=hslot[:, tt % 2, :], in0=xt,
                                scalar1=st("rstd", tt), scalar2=st("negmz", tt),
                                op0=ALU.mult, op1=ALU.add)
                k.ck(("l1h", s, tt), "dv")
                k.wait(t, ("l1h", s, tt))
                if tt == 0:
                    k.wait(t, "ident")
                    if s > 0:
                        k.wait(t, ("ffn_done", s - 1))
                if tt >= 2:
                    k.wait(t, ("l1trev", s, tt - 2))
                for jc in range(8):
                    t.transpose(psT[tt % 2][:, jc * 128:(jc + 1) * 128],
                                hslot[:, tt % 2, jc * 128:(jc + 1) * 128],
                                identb[:], inc=(1 if jc == 7 else 0))
                k.ck(("l1tr", s, tt), "pe")
                k.wait(v, ("l1tr", s, tt))
                v.tensor_copy(out=hT[:, :, tt * 128:(tt + 1) * 128],
                              in_=psT[tt % 2][:, :].rearrange("p (c q) -> p c q", q=128))
                k.ck(("l1trev", s, tt), "dv")
                if (tt + 1) % 4 == 0:
                    nq = tt // 4
                    for ic in range(8):
                        h = ic // 4
                        u = nq * 8 + ic
                        par = u % 2
                        if ic == 0 or ic == 4:
                            k.wait(t, ("wld", s, 0, h))
                        k.wait(t, ("l1trev", s, tt))
                        if u >= 2:
                            k.wait(t, ("qkev", s, u - 2))
                        for jc in range(8):
                            t.matmul(pw[par][:, :],
                                     lhsT=wsl[h][:, jc, (ic % 4) * 128:(ic % 4 + 1) * 128],
                                     rhs=hT[:, jc, nq * 512:(nq + 1) * 512],
                                     start=(jc == 0), stop=(jc == 7),
                                     inc=(1 if jc == 7 else 0))
                        k.ck(("qkmm", s, u), "pe")
                        if nq == NT5 - 1 and (ic == 3 or ic == 7):
                            k.ck(("proj_half", s, 0, ic // 4), "pe")
                            attn_wload(1, ic // 4, ("proj_half", s, 0, ic // 4))
                        k.wait(s_, ("qkmm", s, u))
                        s_.activation(out=qT[:, ic, nq * 512:(nq + 1) * 512],
                                      in_=pw[par][:, :], func=AF.Copy, scale=1.0 / 32.0)
                        k.ck(("qkev", s, u), "sc")
            k.ck(("hT_done", s), "dv")

            if stop_after == "ln1":
                k.wait(g, ("hT_done", s))
                g.dma(out_d[s * P:(s + 1) * P, :].rearrange("p (t d) -> p t d", d=D), residv, "do")
                k.ck(("outd", s), "do")
                return
            for ic in range(8):
                h = ic // 4
                for nq in range(NT5):
                    u = 8 * NT5 + ic * NT5 + nq
                    par = u % 2
                    if nq == 0:
                        k.wait(t, ("wld", s, 1, h))
                    k.wait(t, ("qkev", s, u - 2))
                    for jc in range(8):
                        t.matmul(pw[par][:, :],
                                 lhsT=wsl[h][:, jc, (ic % 4) * 128:(ic % 4 + 1) * 128],
                                 rhs=hT[:, jc, nq * 512:(nq + 1) * 512],
                                 start=(jc == 0), stop=(jc == 7),
                                 inc=(1 if jc == 7 else 0))
                    k.ck(("qkmm", s, u), "pe")
                    if (ic == 3 or ic == 7) and nq == NT5 - 1:
                        k.ck(("proj_half", s, 1, ic // 4), "pe")
                        attn_wload(2, ic // 4, ("proj_half", s, 1, ic // 4))
                    k.wait(v, ("qkmm", s, u))
                    v.tensor_copy(out=kT[:, ic, nq * 512:(nq + 1) * 512], in_=pw[par][:, :])
                    k.ck(("qkev", s, u), "dv")
            k.ck(("q_done", s), "sc")
            k.ck(("k_done", s), "dv")

            k.wait(v, ("q_done", s))
            v.tensor_reduce(out=qmc[:, 0:1], in_=qT.rearrange("p c s -> p (c s)"),
                            op=ALU.max, axis=AX.X, apply_absolute_value=True)
            v.tensor_reduce(out=qmc[:, 1:2], in_=kT.rearrange("p c s -> p (c s)"),
                            op=ALU.max, axis=AX.X, apply_absolute_value=True)
            k.ck(("qkmax", s), "dv")
            k.wait(t, ("qkmax", s))
            k.wait(t, ("qkev", s, 16 * NT5 - 1))
            t.transpose(psT0[0:1, 0:128], qmc[:, 0:1], identb[:], inc=0)
            t.transpose(psT0[0:1, 128:256], qmc[:, 1:2], identb[:], inc=1)
            k.ck(("qkmaxT", s), "pe")
            k.wait(v, ("qkmaxT", s))
            v.tensor_reduce(out=mrow[0:1, 0:1], in_=psT0[0:1, 0:128], op=ALU.max, axis=AX.X)
            v.tensor_reduce(out=mrow[0:1, 1:2], in_=psT0[0:1, 128:256], op=ALU.max, axis=AX.X)
            v.tensor_tensor(out=mrow[0:1, 2:3], in0=mrow[0:1, 0:1], in1=mrow[0:1, 1:2],
                            op=ALU.mult)
            v.tensor_copy(out=mrowb[0:1, 0:1], in_=mrow[0:1, 2:3])
            k.ck(("mprod", s), "dv")
            k.wait(t, ("mprod", s))
            t.matmul(pw0[:, 0:1], lhsT=ones_b1[0:1, :], rhs=mrowb[0:1, 0:1],
                     start=True, stop=True, inc=1)
            k.ck(("mbc", s), "pe")
            k.wait(v, ("mbc", s))
            v.tensor_scalar(out=negM[:], in0=pw0[:, 0:1], scalar1=-64.0, scalar2=None,
                            op0=ALU.mult)
            k.ck(("negM", s), "dv")

            for tt in range(TT):
                par = tt % 2
                for half in range(2):
                    if tt == 0:
                        k.wait(t, ("wld", s, 2, half))
                        if half == 0:
                            k.wait(t, ("mprod", s))
                    if tt >= 2 and half == 0:
                        k.wait(t, ("vev", s, tt - 2))
                    for jc in range(8):
                        t.matmul(pv[par][:, half * 512:(half + 1) * 512],
                                 lhsT=hT[:, jc, tt * 128:(tt + 1) * 128],
                                 rhs=wsl[half][:, jc, :],
                                 start=(jc == 0), stop=(jc == 7),
                                 inc=(1 if (jc == 7 and half == 1) else 0))
                k.ck(("vmm", s, tt), "pe")
                k.wait(v, ("vmm", s, tt))
                k.wait(v, ("vones", s))
                v.tensor_copy(out=vext[:, tt, :, 0:64],
                              in_=pv[par][:, :].rearrange("p (h c) -> p h c", c=64))
                k.ck(("vev", s, tt), "dv")
                if tt == TT - 1:
                    k.ck(("vproj_pe", s), "pe")
                    for h in range(2):
                        attn_wload(3, h, ("vproj_pe", s))
            k.ck(("vproj_done", s), "dv")

            if stop_after == "qkv":
                k.wait(g, ("vproj_done", s))
                k.wait(g, ("negM", s))
                g.dma(out_d[s * P:(s + 1) * P, :].rearrange("p (t d) -> p t d", d=D), residv, "do")
                k.ck(("outd", s), "do")
                return
            first = True
            pu = pqh = 0
            hist_oT = []
            for u in range(NPAIR):
                for qh in range(NQH):
                    qpar = qh % 2
                    NPP = KT // 2
                    for pp in range(NPP + 1):
                        if pp < NPP:
                            for ki in range(2):
                                kt = 2 * pp + ki
                                if kt == 0:
                                    if first:
                                        k.wait(t, ("negM", s))
                                        k.wait(t, ("vproj_done", s))
                                    else:
                                        k.wait(t, ("rbsv", s, pu, pqh, 1))
                                if ki == 0 and pp >= 1:
                                    k.wait(t, ("expp", s, u, qh, pp - 1, 0))
                                    k.wait(t, ("expp", s, u, qh, pp - 1, 1))
                                t.matmul(pA_[:, ki * QW:(ki + 1) * QW],
                                         lhsT=kT[0:64, u, kt * 128:(kt + 1) * 128],
                                         rhs=qT[0:64, u, qh * QW:(qh + 1) * QW],
                                         start=True, stop=True, inc=1)
                                k.ck(("smm", s, u, qh, kt, 0), "pe")
                                t.matmul(pB_[:, ki * QW:(ki + 1) * QW],
                                         lhsT=kT[64:128, u, kt * 128:(kt + 1) * 128],
                                         rhs=qT[64:128, u, qh * QW:(qh + 1) * QW],
                                         start=True, stop=True, inc=1)
                                k.ck(("smm", s, u, qh, kt, 1), "pe")
                            for hb in range(2):
                                k.wait(s_, ("smm", s, u, qh, 2 * pp + 1, hb))
                                if pp >= 2:
                                    k.wait(s_, ("avmm", s, u, qh, 2 * pp - 3, hb))
                                s_.activation(out=abuf[:, pp % 2, hb, :],
                                              in_=pA_[:, :] if hb == 0 else pB_[:, :],
                                              func=AF.Exp, bias=negM[:], scale=1.0)
                                k.ck(("expp", s, u, qh, pp, hb), "sc")
                        if pp >= 1:
                            pa_i = pp - 1
                            for ki in range(2):
                                ka = 2 * pa_i + ki
                                for hb in range(2):
                                    if ki == 0:
                                        k.wait(t, ("expp", s, u, qh, pa_i, hb))
                                    if ka == 0 and len(hist_oT) >= 2:
                                        k.wait(t, ("otev", s) + hist_oT[-2] + (hb,))
                                    pO = (pOA if hb == 0 else pOB)[qpar]
                                    t.matmul(pO[0:65, :],
                                             lhsT=vext[:, ka, 2 * u + hb, :],
                                             rhs=abuf[:, pa_i % 2, hb, ki * QW:(ki + 1) * QW],
                                             start=(ka == 0), stop=(ka == KT - 1), inc=1)
                                    k.ck(("avmm", s, u, qh, ka, hb), "pe")
                    for hb in range(2):
                        pO = (pOA if hb == 0 else pOB)[qpar]
                        pS0 = pA_ if hb == 0 else pB_
                        k.wait(v, ("avmm", s, u, qh, KT - 1, hb))
                        v.reciprocal(out=rbs[0:1, :], in_=pO[64:65, :])
                        v.tensor_copy(out=rrowb[0:1, :], in_=rbs[0:1, :])
                        k.ck(("rcp", s, u, qh, hb), "dv")
                        k.wait(t, ("rcp", s, u, qh, hb))
                        t.matmul(pS0[0:64, 0:QW], lhsT=ones_b1[0:1, 0:64],
                                 rhs=rrowb[0:1, :], start=True, stop=True, inc=1)
                        k.ck(("rbc", s, u, qh, hb), "pe")
                        k.wait(v, ("rbc", s, u, qh, hb))
                        v.tensor_copy(out=rbs[0:64, :], in_=pS0[0:64, 0:QW])
                        k.ck(("rbsv", s, u, qh, hb), "dv")
                        v.tensor_tensor(out=oT[hb * 64:hb * 64 + 64, u, qh * QW:(qh + 1) * QW],
                                        in0=pO[0:64, :], in1=rbs[0:64, :], op=ALU.mult)
                        k.ck(("otev", s, u, qh, hb), "dv")
                    first = False
                    pu, pqh = u, qh
                    hist_oT.append((u, qh))
            k.ck(("oT_done", s), "dv")
            k.ck(("av_done", s), "pe")

            if stop_after == "attn":
                k.wait(g, ("oT_done", s))
                g.dma(out_d[s * P:(s + 1) * P, :].rearrange("p (t d) -> p t d", d=D), residv, "do")
                k.ck(("outd", s), "do")
                return
            for tt in range(TT):
                par = tt % 2
                for half in range(2):
                    if tt == 0:
                        k.wait(t, ("wld", s, 3, half))
                        if half == 0:
                            k.wait(t, ("oT_done", s))
                            k.wait(t, "smload")
                            k.wait(t, "smload_b")
                    if tt >= 2 and half == 0:
                        k.wait(t, ("aoev", s, tt - 2))
                    for jc in range(8):
                        t.matmul(pAO[par][:, half * 512:(half + 1) * 512],
                                 lhsT=oT[:, jc, tt * 128:(tt + 1) * 128],
                                 rhs=wsl[half][:, jc, :],
                                 start=(jc == 0), stop=False, inc=0)
                    t.matmul(pAO[par][:, half * 512:(half + 1) * 512],
                             lhsT=ones_b1[0:1, :],
                             rhs=bffr[0:1, half * 512:(half + 1) * 512],
                             start=False, stop=True,
                             inc=(1 if half == 1 else 0))
                k.ck(("aomm", s, tt), "pe")
                k.wait(v, ("aomm", s, tt))
                v.tensor_tensor(out=residv[:, tt, :], in0=pAO[par][:, :],
                                in1=residv[:, tt, :], op=ALU.add)
                k.ck(("aoev", s, tt), "dv")
            k.ck(("ao_done", s), "dv")

            if stop_after == "ao":
                k.wait(g, ("ao_done", s))
                g.dma(out_d[s * P:(s + 1) * P, :].rearrange("p (t d) -> p t d", d=D), residv, "do")
                k.ck(("outd", s), "do")
                return
            for tt in range(TT):
                xt = residv[:, tt, :]
                k.wait(s_, ("aoev", s, tt))
                s_.activation(out=pdum2[:, :], in_=xt, func=AF.Copy, scale=1.0 / D,
                              accum_out=st("mean", tt))
                s_.activation(out=pdum2[:, :], in_=xt, func=AF.Square, scale=1.0 / 32.0,
                              accum_out=st("ss", tt))
                k.ck(("l2st", s, tt), "sc")
                if stop_after == "l2st":
                    continue
                k.wait(v, ("l2st", s, tt))
                v.tensor_scalar(out=st("msq", tt), in0=st("mean", tt),
                                scalar1=st("mean", tt), scalar2=None, op0=ALU.mult)
                v.tensor_tensor(out=st("var", tt), in0=st("ss", tt), in1=st("msq", tt),
                                op=ALU.subtract)
                k.ck(("l2var", s, tt), "dv")
                k.wait(s_, ("l2var", s, tt))
                s_.activation(out=st("lnv", tt), in_=st("var", tt), func=AF.Ln,
                              bias=epsb[:], scale=1.0)
                s_.activation(out=st("rstd", tt), in_=st("lnv", tt), func=AF.Exp, scale=-0.5)
                k.ck(("l2rstd", s, tt), "sc")
                k.wait(v, ("l2rstd", s, tt))
                v.tensor_scalar(out=st("negmz", tt), in0=st("mean", tt),
                                scalar1=st("rstd", tt), scalar2=-1.0,
                                op0=ALU.mult, op1=ALU.mult)
                if tt >= 1:
                    k.wait(v, ("l2tr", s, tt - 1))
                v.tensor_scalar(out=xln[:], in0=xt, scalar1=st("rstd", tt),
                                scalar2=st("negmz", tt), op0=ALU.mult, op1=ALU.add)
                k.ck(("l2h", s, tt), "dv")
                if stop_after == "l2h":
                    continue
                k.wait(t, ("l2h", s, tt))
                if tt == 0:
                    k.wait(t, ("ao_done", s))
                if tt >= 2 and stop_after != "l2trx":
                    k.wait(t, ("l2trev", s, tt - 2))
                    k.wait(t, ("l2trev_f", s, tt - 2))
                for jc in range(8):
                    t.transpose(pxT[tt % 2][:, jc * 128:(jc + 1) * 128],
                                xln[:, jc * 128:(jc + 1) * 128], identf[:],
                                inc=(1 if jc == 7 else 0))
                k.ck(("l2tr", s, tt), "pe")
                if stop_after == "l2trx":
                    continue
                k.wait(v, ("l2tr", s, tt))
                if tt >= 1 and stop_after != "ln2tr":
                    k.wait(v, ("rmm", s, tt - 1))
                v.tensor_copy(out=xlnT[:], in_=pxT[tt % 2][:, :])
                k.ck(("l2trev_f", s, tt), "dv")
                k.wait(s_, ("l2tr", s, tt))
                k.wait(s_, ("l2trev_f", s, tt))
                s_.activation(out=x2T[:, :, tt * 128:(tt + 1) * 128],
                              in_=pxT[tt % 2][:, :].rearrange("p (c q) -> p c q", q=128),
                              func=AF.Copy, scale=1.0)
                k.ck(("l2trev", s, tt), "sc")
                if stop_after == "ln2tr":
                    continue
                k.wait(t, ("l2trev_f", s, tt))
                if tt == 0:
                    k.wait(t, ("oT_done", s))
                if tt >= 2 and stop_after not in ("rmm", "rsum"):
                    k.wait(t, ("rmsk", s, tt - 2))
                for jc in range(8):
                    t.matmul(pL[tt % 2][:, :], lhsT=xlnT[:, jc * 128:(jc + 1) * 128],
                             rhs=rwv[:, jc, :], start=(jc == 0), stop=False, inc=0)
                t.matmul(pL[tt % 2][:, :], lhsT=ones_f1[0:1, :], rhs=rbr[0:1, :],
                         start=False, stop=True, inc=1)
                k.ck(("rmm", s, tt), "pe")
                if stop_after == "rmm":
                    continue
                k.wait(v, ("rmm", s, tt))
                v.tensor_reduce(out=st("maxl", tt), in_=pL[tt % 2][:, :], op=ALU.max,
                                axis=AX.X)
                v.tensor_scalar(out=st("negml", tt), in0=st("maxl", tt), scalar1=-1.0,
                                scalar2=None, op0=ALU.mult)
                k.ck(("rmax", s, tt), "dv")
                k.wait(s_, ("rmax", s, tt))
                s_.activation(out=elb[:], in_=pL[tt % 2][:, :], func=AF.Exp,
                              bias=st("negml", tt), scale=1.0, accum_out=st("sumr", tt))
                k.ck(("rsum", s, tt), "sc")
                if stop_after == "rsum":
                    continue
                k.wait(v, ("rsum", s, tt))
                v.reciprocal(out=st("mp", tt), in_=st("sumr", tt))
                v.tensor_scalar(out=msk[:], in0=pL[tt % 2][:, :], scalar1=st("maxl", tt),
                                scalar2=None, op0=ALU.is_equal)
                v.tensor_scalar(out=sctab[:, tt, :], in0=msk[:], scalar1=st("mp", tt),
                                scalar2=None, op0=ALU.mult)
                k.ck(("rmsk", s, tt), "dv")
            k.ck(("router_done", s), "dv")

            if stop_after in ("router", "ln2tr", "rmm", "rsum", "l2st", "l2h", "l2trx"):
                k.wait(g, ("router_done", s))
                g.dma(out_d[s * P:(s + 1) * P, :].rearrange("p (t d) -> p t d", d=D), residv, "do")
                k.ck(("outd", s), "do")
                return
            for gidx0 in range(2):
                ei, gi = g_list[gidx0]
                k.wait(g, ("ao_done", s))
                g.dma(wsl[gidx0 % 2],
                      w1_in[(ei * 16 + gi) * P:(ei * 16 + gi + 1) * P, :]
                      .rearrange("p (c f) -> p c f", f=512),
                      "dwa" if gidx0 % 2 == 0 else "dwb")
                k.ck(("w1ld", s, gidx0), "dwa" if gidx0 % 2 == 0 else "dwb")
            for zidx0 in range(2):
                ei, qi = z_list[zidx0]
                k.wait(g, ("router_done", s))
                g.dma(w2sl[zidx0 % 2],
                      w2_in[(ei * 4 + qi) * P:(ei * 4 + qi + 1) * P, :]
                      .rearrange("p (c f) -> p c f", f=256),
                      "dza" if zidx0 % 2 == 0 else "dzb")
                k.ck(("w2ld", s, zidx0), "dza" if zidx0 % 2 == 0 else "dzb")

            gidx = 0
            zidx = 0
            pprev = None
            py_hist = []
            for m in range(NM):
                for ei in range(NE):
                    for gi in range(16):
                        slot = gidx % 2
                        for i2 in range(2):
                            i = 2 * gi + i2
                            if i2 == 0:
                                k.wait(t, ("w1ld", s, gidx))
                            if gidx == 0 and i2 == 0 and m == 0:
                                k.wait(t, ("router_done", s))
                                k.wait(t, ("l2trev", s, TT - 1))
                            if pprev is not None:
                                k.wait(t, ("silu", s, *pprev))
                            nw = min(512, MW)
                            for n in range(0, MW, nw):
                                for jc in range(8):
                                    t.matmul(pa[:, n:n + nw],
                                             lhsT=wsl[slot][:, jc, i2 * 128:(i2 + 1) * 128],
                                             rhs=x2T[:, jc, m * MW + n:m * MW + n + nw],
                                             start=(jc == 0), stop=(jc == 7),
                                             inc=(1 if (jc == 7 and n + nw == MW) else 0))
                            k.ck(("p1a", s, m, ei, i), "pe")
                            if pprev is not None:
                                k.wait(t, ("hmul", s, *pprev))
                            for n in range(0, MW, nw):
                                for jc in range(8):
                                    t.matmul(pg_[:, n:n + nw],
                                             lhsT=wsl[slot][:, jc, 256 + i2 * 128:256 + (i2 + 1) * 128],
                                             rhs=x2T[:, jc, m * MW + n:m * MW + n + nw],
                                             start=(jc == 0), stop=(jc == 7),
                                             inc=(1 if (jc == 7 and n + nw == MW) else 0))
                            k.ck(("p1g", s, m, ei, i), "pe")
                            if i2 == 1:
                                k.ck(("p1cons", s, gidx), "pe")
                            k.wait(s_, ("p1a", s, m, ei, i))
                            if pprev is not None:
                                k.wait(s_, ("hmul", s, *pprev))
                            s_.activation(out=hslot[:, 0, 0:MW], in_=pa[:, :], func=AF.Silu,
                                          scale=1.0)
                            k.ck(("silu", s, m, ei, i), "sc")
                            k.wait(v, ("silu", s, m, ei, i))
                            k.wait(v, ("p1g", s, m, ei, i))
                            v.tensor_tensor(out=hh[:, i, :], in0=pg_[:, :],
                                            in1=hslot[:, 0, 0:MW], op=ALU.mult)
                            k.ck(("hmul", s, m, ei, i), "dv")
                            pprev = (m, ei, i)
                        if gidx + 2 < NG1:
                            nei, ngi = g_list[gidx + 2]
                            k.wait(g, ("p1cons", s, gidx))
                            g.dma(wsl[gidx % 2],
                                  w1_in[(nei * 16 + ngi) * P:(nei * 16 + ngi + 1) * P, :]
                                  .rearrange("p (c f) -> p c f", f=512),
                                  "dwa" if gidx % 2 == 0 else "dwb")
                            k.ck(("w1ld", s, gidx + 2), "dwa" if gidx % 2 == 0 else "dwb")
                        gidx += 1
                    for qi in range(4):
                        slot = zidx % 2
                        for tc in range(MTC):
                            par = tc % 2
                            gt = m * MTC + tc
                            if tc == 0:
                                k.wait(t, ("w2ld", s, zidx))
                            if qi == 0 and tc == 0:
                                k.wait(t, ("hmul", s, m, ei, 31))
                                if m == 0 and ei == 0:
                                    k.wait(t, ("rmsk", s, TT - 1))
                            if len(py_hist) >= 2:
                                k.wait(t, ("p2ev", s, *py_hist[-2]))
                            for i in range(32):
                                t.matmul(py[par][:, 0:256],
                                         lhsT=hh[:, i, tc * 128:(tc + 1) * 128],
                                         rhs=w2sl[slot][:, i, :],
                                         start=(i == 0), stop=(i == 31),
                                         inc=(1 if i == 31 else 0))
                            k.ck(("p2mm", s, m, ei, qi, tc), "pe")
                            if tc == MTC - 1:
                                k.ck(("p2cons", s, zidx), "pe")
                            k.wait(v, ("p2mm", s, m, ei, qi, tc))
                            v.scalar_tensor_tensor(
                                out=residv[:, gt, qi * 256:(qi + 1) * 256],
                                in0=py[par][:, 0:256],
                                scalar=sctab[:, gt, ei:ei + 1],
                                in1=residv[:, gt, qi * 256:(qi + 1) * 256],
                                op0=ALU.mult, op1=ALU.add)
                            k.ck(("p2ev", s, m, ei, qi, tc), "dv")
                            py_hist.append((m, ei, qi, tc))
                        if zidx + 2 < NZ2:
                            nei, nqi = z_list[zidx + 2]
                            k.wait(g, ("p2cons", s, zidx))
                            g.dma(w2sl[zidx % 2],
                                  w2_in[(nei * 4 + nqi) * P:(nei * 4 + nqi + 1) * P, :]
                                  .rearrange("p (c f) -> p c f", f=256),
                                  "dza" if zidx % 2 == 0 else "dzb")
                            k.ck(("w2ld", s, zidx + 2), "dza" if zidx % 2 == 0 else "dzb")
                        zidx += 1
                    if ei == NE - 1:
                        k.ck(("m_done", s, m), "dv")
                        k.wait(g, ("m_done", s, m))
                        g.dma(out_d[s * P:(s + 1) * P, :].rearrange("p (t d) -> p t d", d=D)
                              [:, m * MTC:(m + 1) * MTC, :],
                              residv[:, m * MTC:(m + 1) * MTC, :], "do")
                        k.ck(("outd", s, m), "do")
            k.ck(("ffn_done", s), "dv")

        ENGS = [("t", "pe", False), ("v", "dv", True), ("s_", "sc", True),
                ("g", "gp", True), ("sy", "sp", False)]

        kd = K(dry=True)
        margs = {n: E(kd, None, sm, ch) for n, sm, ch in ENGS}
        emit(kd, margs["t"], margs["v"], margs["s_"], margs["g"], margs["sy"])
        chk = kd.chk
        dry_cnt = dict(kd.cnt)

        def real_pass(name):
            def body(eng):
                k = K(dry=False, sems=sems, chk=chk)
                args = {n: E(k, None, sm, ch) for n, sm, ch in ENGS}
                for n, sm, ch in ENGS:
                    if n == name:
                        args[n] = E(k, eng, sm, ch)
                emit(k, args["t"], args["v"], args["s_"], args["g"], args["sy"])
                for snm, val in dry_cnt.items():
                    assert k.cnt[snm] == val, (name, snm, k.cnt[snm], val)
            return body

        block.tensor(real_pass("t"))
        block.vector(real_pass("v"))
        block.scalar(real_pass("s_"))
        block.gpsimd(real_pass("g"))
        block.sync(real_pass("sy"))

    return nc


_NC_CACHE = {}


def _relayout(L0_attn_Wq, L0_attn_Wk, L0_attn_Wv, L0_attn_Wff, L0_attn_bff,
              L0_ffn_router_w, L0_ffn_router_b, L0_ffn_w1, L0_ffn_w2):
    c = np.ascontiguousarray
    f32 = np.float32

    def wrel(W):
        return c(np.asarray(W, f32).T.reshape(8, 128, D).transpose(1, 0, 2)
                 .reshape(128, 8, 2, 512).transpose(0, 2, 1, 3)).reshape(128, 8192)

    return dict(
        wq=wrel(L0_attn_Wq), wk=wrel(L0_attn_Wk), wv=wrel(L0_attn_Wv),
        wf=wrel(L0_attn_Wff),
        bff=c(np.asarray(L0_attn_bff, f32)).reshape(1, D),
        rw=c(np.asarray(L0_ffn_router_w, f32).reshape(8, 128, NE)
             .transpose(1, 0, 2)).reshape(128, 8 * NE),
        rb=c(np.asarray(L0_ffn_router_b, f32)).reshape(1, NE),
        w1=c(np.asarray(L0_ffn_w1, f32).reshape(NE, 8, 128, 2, 16, 256)
             .transpose(0, 4, 2, 1, 3, 5)).reshape(NE * 16 * 128, 8 * 512),
        w2=c(np.asarray(L0_ffn_w2, f32).reshape(NE, 32, 128, 4, 256)
             .transpose(0, 3, 2, 1, 4)).reshape(NE * 4 * 128, 32 * 256),
    )


def kernel(L0_attn_x, L0_attn_Wq, L0_attn_Wk, L0_attn_Wv, L0_attn_Wff,
           L0_attn_bff, L0_ffn_router_w, L0_ffn_router_b, L0_ffn_w1,
           L0_ffn_w2):
    x = np.ascontiguousarray(np.asarray(L0_attn_x, dtype=np.float32))
    B, S, D_ = x.shape
    nseq = B // NCORES
    TT = S // 128
    key = (S, nseq)
    if key not in _NC_CACHE:
        _NC_CACHE[key] = build_program(S=S, NSEQ=nseq)
    nc = _NC_CACHE[key]

    wmap = _relayout(L0_attn_Wq, L0_attn_Wk, L0_attn_Wv, L0_attn_Wff,
                     L0_attn_bff, L0_ffn_router_w, L0_ffn_router_b,
                     L0_ffn_w1, L0_ffn_w2)

    in_maps = []
    for cc in range(NCORES):
        xs = np.ascontiguousarray(
            x[cc * nseq:(cc + 1) * nseq].reshape(nseq, TT, 128, D_)
            .transpose(0, 2, 1, 3)).reshape(nseq * 128, TT * D_)
        mp = dict(wmap)
        mp["x"] = xs
        in_maps.append(mp)

    res = run_bass_kernel_spmd(nc, in_maps, core_ids=list(range(NCORES)))
    out = np.empty((B, S, D_), np.float32)
    for cc in range(NCORES):
        o = res.results[cc]["out"].reshape(nseq, 128, TT, D_).transpose(0, 2, 1, 3)
        out[cc * nseq:(cc + 1) * nseq] = o.reshape(nseq, S, D_)
    return out


# version 10
# speedup vs baseline: 1.0700x; 1.0700x over previous
import numpy as np
from contextlib import ExitStack
from collections import defaultdict

import concourse.bass as bass
import concourse.mybir as mybir
from concourse.bass_utils import run_bass_kernel_spmd

F32 = mybir.dt.float32
BF16 = mybir.dt.bfloat16
I32 = mybir.dt.int32
AF = mybir.ActivationFunctionType
ALU = mybir.AluOpType
AX = mybir.AxisListType

P = 128
D = 1024
NH = 16
NE = 8
FH = 4096
NCORES = 8


class MockInst:
    def then_inc(self, *a, **k):
        return self


class K:

    def __init__(self, dry, sems=None, chk=None):
        self.dry = dry
        self.sems = sems or {}
        self.cnt = defaultdict(int)
        self.chk = chk if chk is not None else {}

    def inc(self, inst, sem, n):
        self.cnt[sem] += n
        if not isinstance(inst, MockInst):
            inst.then_inc(self.sems[sem], n)

    def ck(self, key, sem):
        v = (sem, self.cnt[sem])
        if self.dry:
            assert key not in self.chk, f"dup checkpoint {key}"
            self.chk[key] = v
        else:
            assert self.chk[key] == v, (key, self.chk[key], v)

    def wait(self, eng, key):
        assert key in self.chk, f"forward wait on {key}"
        if eng.real is None:
            return
        sem, val = self.chk[key]
        if val > 0:
            eng.real.wait_ge(self.sems[sem], val)


class E:

    def __init__(self, k, real, sem, chain):
        self.k = k
        self.real = real
        self.sem = sem
        self.chain = chain

    def __getattr__(self, name):
        def f(*a, inc=None, **kw):
            k = self.k
            if self.chain:
                prev = k.cnt[self.sem]
                if prev > 0 and self.real is not None:
                    self.real.wait_ge(k.sems[self.sem], prev)
            inst = (getattr(self.real, name)(*a, **kw)
                    if self.real is not None else MockInst())
            n = inc if inc is not None else (1 if self.chain else 0)
            if n:
                k.inc(inst, self.sem, n)
            return inst
        return f

    def dma(self, out, in_, sem):
        k = self.k
        inst = (self.real.dma_start(out=out, in_=in_)
                if self.real is not None else MockInst())
        k.inc(inst, sem, 16)
        return inst


def build_program(S=2048, NSEQ=2, stop_after=None):
    TT = S // 128
    KT = TT
    QW = min(512, S)
    NQH = S // QW
    NT5 = S // 512
    MW = min(1024, S // 2)
    NM = S // MW
    MTC = MW // 128
    NPAIR = NH // 2
    NG1 = NM * NE * 16
    NZ2 = NM * NE * 4
    g_list = [(ei, gi) for m in range(NM) for ei in range(NE) for gi in range(16)]
    z_list = [(ei, qi) for m in range(NM) for ei in range(NE) for qi in range(4)]

    nc = bass.Bass("TRN2", target_bir_lowering=False)

    x_in = nc.dram_tensor("x", [NSEQ * P, TT * D], F32, kind="ExternalInput").ap()
    wq_in = nc.dram_tensor("wq", [P, 8192], F32, kind="ExternalInput").ap()
    wk_in = nc.dram_tensor("wk", [P, 8192], F32, kind="ExternalInput").ap()
    wv_in = nc.dram_tensor("wv", [P, 8192], F32, kind="ExternalInput").ap()
    wf_in = nc.dram_tensor("wf", [P, 8192], F32, kind="ExternalInput").ap()
    bff_in = nc.dram_tensor("bff", [1, D], F32, kind="ExternalInput").ap()
    rw_in = nc.dram_tensor("rw", [P, 64], F32, kind="ExternalInput").ap()
    rb_in = nc.dram_tensor("rb", [1, NE], F32, kind="ExternalInput").ap()
    w1_in = nc.dram_tensor("w1", [NE * 16 * P, 8 * 512], F32, kind="ExternalInput").ap()
    w2_in = nc.dram_tensor("w2", [NE * 4 * P, 32 * 256], F32, kind="ExternalInput").ap()
    out_d = nc.dram_tensor("out", [NSEQ * P, TT * D], F32, kind="ExternalOutput").ap()
    W_INS = [wq_in, wk_in, wv_in, wf_in]

    with ExitStack() as top:
        e = top.enter_context

        resid = e(nc.sbuf_tensor("resid", [P, TT, D], BF16))
        harena = e(nc.sbuf_tensor("harena", [P, 8, S], BF16))
        qkarena = e(nc.sbuf_tensor("qkarena", [P, 16, S], BF16))
        vext = e(nc.sbuf_tensor("vext", [P, TT, NH, 65], BF16))
        wbuf = e(nc.sbuf_tensor("wbuf", [P, 2, 8, 512], BF16))
        abuf = e(nc.sbuf_tensor("abuf", [P, 2, 2, 2 * QW], BF16))
        hslot = e(nc.sbuf_tensor("hslot", [P, 2, D], BF16))
        rbs = e(nc.sbuf_tensor("rbs", [64, QW], F32))
        rrowb = e(nc.sbuf_tensor("rrowb", [1, QW], BF16))
        xln = e(nc.sbuf_tensor("xln", [P, D], F32))
        xlnT = e(nc.sbuf_tensor("xlnT", [P, D], F32))
        stats = e(nc.sbuf_tensor("stats", [P, 24], F32))
        msk = e(nc.sbuf_tensor("msk", [P, NE], F32))
        elb = e(nc.sbuf_tensor("elb", [P, NE], F32))
        sctab = e(nc.sbuf_tensor("sctab", [P, TT, NE], F32))
        rwv = e(nc.sbuf_tensor("rwv", [P, 8, NE], F32))
        rbr = e(nc.sbuf_tensor("rbr", [1, NE], F32))
        bffr = e(nc.sbuf_tensor("bffr", [1, D], BF16))
        identb = e(nc.sbuf_tensor("identb", [P, P], BF16))
        identf = e(nc.sbuf_tensor("identf", [P, P], F32))
        ones_b1 = e(nc.sbuf_tensor("ones_b1", [1, P], BF16))
        ones_f1 = e(nc.sbuf_tensor("ones_f1", [1, P], F32))
        negM = e(nc.sbuf_tensor("negM", [P, 1], F32))
        epsb = e(nc.sbuf_tensor("epsb", [P, 1], F32))
        qmc = e(nc.sbuf_tensor("qmc", [P, 2], BF16))
        mrow = e(nc.sbuf_tensor("mrow", [1, 8], F32))
        mrowb = e(nc.sbuf_tensor("mrowb", [1, 1], BF16))
        iota1 = e(nc.sbuf_tensor("iota1", [P, P], I32))
        iota2 = e(nc.sbuf_tensor("iota2", [P, P], I32))

        residv = resid[:, :, :]
        hT = harena[:, :, :]
        oT = hT
        x2T = hT
        qT = qkarena[:, 0:8, :]
        kT = qkarena[:, 8:16, :]
        hh = qkarena[:, :, :].rearrange("p c s -> p (c s)")[:, 0:32 * MW].rearrange(
            "p (c m) -> p c m", m=MW)
        if TT * NH * 65 >= 16384:
            w2sl = [vext[:, :, :, :].rearrange("p a b c -> p (a b c)")[:, kk * 8192:(kk + 1) * 8192]
                    .rearrange("p (c f) -> p c f", f=256) for kk in range(2)]
        else:
            w2b = e(nc.sbuf_tensor("w2b", [P, 2, 32, 256], BF16))
            w2sl = [w2b[:, 0, :, :], w2b[:, 1, :, :]]
        wsl = [wbuf[:, 0, :, :], wbuf[:, 1, :, :]]

        SC = {n: i for i, n in enumerate(
            ["mean0", "mean1", "ss0", "ss1", "msq0", "msq1", "var0", "var1",
             "lnv0", "lnv1", "rstd0", "rstd1", "negmz0", "negmz1",
             "maxl0", "maxl1", "negml0", "negml1", "sumr0", "sumr1",
             "mp0", "mp1"])}

        def st(name, par=None):
            i = SC[name if par is None else name + str(par % 2)]
            return stats[:, i:i + 1]

        def pgroup(*specs):
            hs = []
            with ExitStack() as stx:
                for nm, shp, dt in specs:
                    hs.append(stx.enter_context(nc.psum_tensor(nm, shp, dt)))
            return hs

        psT0, psT1, pdum1, pw0, pw1 = pgroup(
            ("psT0", [P, D], BF16), ("psT1", [P, D], BF16), ("pdum1", [P, D], F32),
            ("pw0", [P, 512], F32), ("pw1", [P, 512], F32))
        pv0, pv1 = pgroup(("pv0", [P, D], F32), ("pv1", [P, D], F32))
        pA_, pB_, pOA0, pOA1, pOB0, pOB1 = pgroup(
            ("pA_", [P, 2 * QW], F32), ("pB_", [P, 2 * QW], F32),
            ("pOA0", [P, QW], F32), ("pOA1", [P, QW], F32),
            ("pOB0", [P, QW], F32), ("pOB1", [P, QW], F32))
        pOA = [pOA0, pOA1]
        pOB = [pOB0, pOB1]
        pAO0, pAO1 = pgroup(("pAO0", [P, D], F32), ("pAO1", [P, D], F32))
        pxT0, pxT1, pL0, pL1, pdum2 = pgroup(
            ("pxT0", [P, D], F32), ("pxT1", [P, D], F32),
            ("pL0", [P, NE], F32), ("pL1", [P, NE], F32), ("pdum2", [P, D], F32))
        pa, pg_, py0, py1 = pgroup(("pa", [P, MW], F32), ("pg", [P, MW], F32),
                                   ("py0", [P, 512], F32), ("py1", [P, 512], F32))
        psT = [psT0, psT1]
        pw = [pw0, pw1]
        pv = [pv0, pv1]
        pAO = [pAO0, pAO1]
        pxT = [pxT0, pxT1]
        pL = [pL0, pL1]
        py = [py0, py1]

        sem_names = ["dx", "dwa", "dwb", "dza", "dzb", "dsm", "dsb", "do",
                     "pe", "dv", "sc", "gp", "sp"]
        sems = {n: e(nc.semaphore(n)) for n in sem_names}

        block = e(nc.Block())

        def emit(k, t, v, s_, g, sy):
            g.iota(iota1[:], pattern=[[1, P]], base=0, channel_multiplier=0)
            g.iota(iota2[:], pattern=[[0, P]], base=0, channel_multiplier=1)
            g.memset(ones_b1[:], 1.0)
            g.memset(ones_f1[:], 1.0)
            g.memset(epsb[:], 1e-5)
            k.ck("init_gp", "gp")
            k.wait(v, "init_gp")
            v.tensor_tensor(out=identb[:], in0=iota1[:], in1=iota2[:], op=ALU.is_equal)
            v.tensor_copy(out=identf[:], in_=identb[:])
            k.ck("ident", "dv")
            sy.dma(rwv[:, :, :], rw_in[:, :].rearrange("p (c e) -> p c e", e=NE), "dsm")
            sy.dma(rbr[:, :], rb_in[:, :], "dsm")
            g.dma(bffr[:, :], bff_in[:, :], "dsb")
            k.ck("smload", "dsm")
            k.ck("smload_b", "dsb")

            for s in range(NSEQ):
                emit_seq(k, t, v, s_, g, sy, s)

            if g.real is not None:
                g.real.wait_ge(k.sems["do"], 16 * NSEQ * NM)

        def emit_seq(k, t, v, s_, g, sy, s):
            HTT = TT // 2
            for hx in range(2):
                if s > 0:
                    k.wait(g, ("outd", s - 1, hx))
                g.dma(residv[:, hx * HTT:(hx + 1) * HTT, :],
                      x_in[s * P:(s + 1) * P, :].rearrange("p (t d) -> p t d", d=D)
                      [:, hx * HTT:(hx + 1) * HTT, :], "dx")
                k.ck(("xld", s, hx), "dx")

            def attn_wload(wi, h, wkey):
                dws = "dwa" if h == 0 else "dwb"
                k.wait(g, wkey)
                g.dma(wsl[h],
                      W_INS[wi][:, :].rearrange("p (h c f) -> p h c f", h=2, f=512)[:, h],
                      dws)
                k.ck(("wld", s, wi, h), dws)

            for h in range(2):
                attn_wload(0, h, ("ffn_done", s - 1) if s > 0 else "init_gp")
            if s > 0:
                k.wait(g, ("ffn_done", s - 1))
            g.memset(vext[:, :, :, 64:65], 1.0)
            k.ck(("vones", s), "gp")

            for tt in range(TT):
                xt = residv[:, tt, :]
                if tt % HTT == 0:
                    k.wait(s_, ("xld", s, tt // HTT))
                    if tt == 0 and s > 0:
                        k.wait(s_, ("ffn_done", s - 1))
                s_.activation(out=pdum1[:, :], in_=xt, func=AF.Copy, scale=1.0 / D,
                              accum_out=st("mean", tt))
                s_.activation(out=pdum1[:, :], in_=xt, func=AF.Square, scale=1.0 / 32.0,
                              accum_out=st("ss", tt))
                k.ck(("l1st", s, tt), "sc")
                if tt % HTT == 0:
                    k.wait(v, ("xld", s, tt // HTT))
                    if tt == 0 and s > 0:
                        k.wait(v, ("ffn_done", s - 1))
                k.wait(v, ("l1st", s, tt))
                v.tensor_scalar(out=st("msq", tt), in0=st("mean", tt),
                                scalar1=st("mean", tt), scalar2=None, op0=ALU.mult)
                v.tensor_tensor(out=st("var", tt), in0=st("ss", tt), in1=st("msq", tt),
                                op=ALU.subtract)
                k.ck(("l1var", s, tt), "dv")
                k.wait(s_, ("l1var", s, tt))
                s_.activation(out=st("lnv", tt), in_=st("var", tt), func=AF.Ln,
                              bias=epsb[:], scale=1.0)
                s_.activation(out=st("rstd", tt), in_=st("lnv", tt), func=AF.Exp, scale=-0.5)
                k.ck(("l1rstd", s, tt), "sc")
                k.wait(v, ("l1rstd", s, tt))
                v.tensor_scalar(out=st("negmz", tt), in0=st("mean", tt),
                                scalar1=st("rstd", tt), scalar2=-1.0,
                                op0=ALU.mult, op1=ALU.mult)
                if tt >= 2:
                    k.wait(v, ("l1tr", s, tt - 2))
                v.tensor_scalar(out=hslot[:, tt % 2, :], in0=xt,
                                scalar1=st("rstd", tt), scalar2=st("negmz", tt),
                                op0=ALU.mult, op1=ALU.add)
                k.ck(("l1h", s, tt), "dv")
                k.wait(t, ("l1h", s, tt))
                if tt == 0:
                    k.wait(t, "ident")
                    if s > 0:
                        k.wait(t, ("ffn_done", s - 1))
                if tt >= 2:
                    k.wait(t, ("l1trev", s, tt - 2))
                for jc in range(8):
                    t.transpose(psT[tt % 2][:, jc * 128:(jc + 1) * 128],
                                hslot[:, tt % 2, jc * 128:(jc + 1) * 128],
                                identb[:], inc=(1 if jc == 7 else 0))
                k.ck(("l1tr", s, tt), "pe")
                k.wait(v, ("l1tr", s, tt))
                v.tensor_copy(out=hT[:, :, tt * 128:(tt + 1) * 128],
                              in_=psT[tt % 2][:, :].rearrange("p (c q) -> p c q", q=128))
                k.ck(("l1trev", s, tt), "dv")
                if (tt + 1) % 4 == 0:
                    nq = tt // 4
                    for ic in range(8):
                        h = ic // 4
                        u = nq * 8 + ic
                        par = u % 2
                        if ic == 0 or ic == 4:
                            k.wait(t, ("wld", s, 0, h))
                        k.wait(t, ("l1trev", s, tt))
                        if u >= 2:
                            k.wait(t, ("qkev", s, u - 2))
                        for jc in range(8):
                            t.matmul(pw[par][:, :],
                                     lhsT=wsl[h][:, jc, (ic % 4) * 128:(ic % 4 + 1) * 128],
                                     rhs=hT[:, jc, nq * 512:(nq + 1) * 512],
                                     start=(jc == 0), stop=(jc == 7),
                                     inc=(1 if jc == 7 else 0))
                        k.ck(("qkmm", s, u), "pe")
                        if nq == NT5 - 1 and (ic == 3 or ic == 7):
                            k.ck(("proj_half", s, 0, ic // 4), "pe")
                            attn_wload(1, ic // 4, ("proj_half", s, 0, ic // 4))
                        k.wait(s_, ("qkmm", s, u))
                        s_.activation(out=qT[:, ic, nq * 512:(nq + 1) * 512],
                                      in_=pw[par][:, :], func=AF.Copy, scale=1.0 / 32.0)
                        k.ck(("qkev", s, u), "sc")
            k.ck(("hT_done", s), "dv")

            if stop_after == "ln1":
                k.wait(g, ("hT_done", s))
                g.dma(out_d[s * P:(s + 1) * P, :].rearrange("p (t d) -> p t d", d=D), residv, "do")
                k.ck(("outd", s), "do")
                return
            for ic in range(8):
                h = ic // 4
                for nq in range(NT5):
                    u = 8 * NT5 + ic * NT5 + nq
                    par = u % 2
                    if nq == 0:
                        k.wait(t, ("wld", s, 1, h))
                    k.wait(t, ("qkev", s, u - 2))
                    for jc in range(8):
                        t.matmul(pw[par][:, :],
                                 lhsT=wsl[h][:, jc, (ic % 4) * 128:(ic % 4 + 1) * 128],
                                 rhs=hT[:, jc, nq * 512:(nq + 1) * 512],
                                 start=(jc == 0), stop=(jc == 7),
                                 inc=(1 if jc == 7 else 0))
                    k.ck(("qkmm", s, u), "pe")
                    if (ic == 3 or ic == 7) and nq == NT5 - 1:
                        k.ck(("proj_half", s, 1, ic // 4), "pe")
                        attn_wload(2, ic // 4, ("proj_half", s, 1, ic // 4))
                    k.wait(v, ("qkmm", s, u))
                    v.tensor_copy(out=kT[:, ic, nq * 512:(nq + 1) * 512], in_=pw[par][:, :])
                    k.ck(("qkev", s, u), "dv")
            k.ck(("q_done", s), "sc")
            k.ck(("k_done", s), "dv")

            k.wait(v, ("q_done", s))
            v.tensor_reduce(out=qmc[:, 0:1], in_=qT.rearrange("p c s -> p (c s)"),
                            op=ALU.max, axis=AX.X, apply_absolute_value=True)
            v.tensor_reduce(out=qmc[:, 1:2], in_=kT.rearrange("p c s -> p (c s)"),
                            op=ALU.max, axis=AX.X, apply_absolute_value=True)
            k.ck(("qkmax", s), "dv")
            k.wait(t, ("qkmax", s))
            k.wait(t, ("qkev", s, 16 * NT5 - 1))
            t.transpose(psT0[0:1, 0:128], qmc[:, 0:1], identb[:], inc=0)
            t.transpose(psT0[0:1, 128:256], qmc[:, 1:2], identb[:], inc=1)
            k.ck(("qkmaxT", s), "pe")
            k.wait(v, ("qkmaxT", s))
            v.tensor_reduce(out=mrow[0:1, 0:1], in_=psT0[0:1, 0:128], op=ALU.max, axis=AX.X)
            v.tensor_reduce(out=mrow[0:1, 1:2], in_=psT0[0:1, 128:256], op=ALU.max, axis=AX.X)
            v.tensor_tensor(out=mrow[0:1, 2:3], in0=mrow[0:1, 0:1], in1=mrow[0:1, 1:2],
                            op=ALU.mult)
            v.tensor_copy(out=mrowb[0:1, 0:1], in_=mrow[0:1, 2:3])
            k.ck(("mprod", s), "dv")
            k.wait(t, ("mprod", s))
            t.matmul(pw0[:, 0:1], lhsT=ones_b1[0:1, :], rhs=mrowb[0:1, 0:1],
                     start=True, stop=True, inc=1)
            k.ck(("mbc", s), "pe")
            k.wait(v, ("mbc", s))
            v.tensor_scalar(out=negM[:], in0=pw0[:, 0:1], scalar1=-64.0, scalar2=None,
                            op0=ALU.mult)
            k.ck(("negM", s), "dv")

            for tt in range(TT):
                par = tt % 2
                for half in range(2):
                    if tt == 0:
                        k.wait(t, ("wld", s, 2, half))
                        if half == 0:
                            k.wait(t, ("mprod", s))
                    if tt >= 2 and half == 0:
                        k.wait(t, ("vev", s, tt - 2))
                    for jc in range(8):
                        t.matmul(pv[par][:, half * 512:(half + 1) * 512],
                                 lhsT=hT[:, jc, tt * 128:(tt + 1) * 128],
                                 rhs=wsl[half][:, jc, :],
                                 start=(jc == 0), stop=(jc == 7),
                                 inc=(1 if (jc == 7 and half == 1) else 0))
                k.ck(("vmm", s, tt), "pe")
                k.wait(v, ("vmm", s, tt))
                k.wait(v, ("vones", s))
                v.tensor_copy(out=vext[:, tt, :, 0:64],
                              in_=pv[par][:, :].rearrange("p (h c) -> p h c", c=64))
                k.ck(("vev", s, tt), "dv")
                if tt == TT - 1:
                    k.ck(("vproj_pe", s), "pe")
                    for h in range(2):
                        attn_wload(3, h, ("vproj_pe", s))
            k.ck(("vproj_done", s), "dv")

            if stop_after == "qkv":
                k.wait(g, ("vproj_done", s))
                k.wait(g, ("negM", s))
                g.dma(out_d[s * P:(s + 1) * P, :].rearrange("p (t d) -> p t d", d=D), residv, "do")
                k.ck(("outd", s), "do")
                return
            first = True
            pu = pqh = 0
            hist_oT = []
            for u in range(NPAIR):
                for qh in range(NQH):
                    qpar = qh % 2
                    NPP = KT // 2
                    for pp in range(NPP + 1):
                        if pp < NPP:
                            for ki in range(2):
                                kt = 2 * pp + ki
                                if kt == 0:
                                    if first:
                                        k.wait(t, ("negM", s))
                                        k.wait(t, ("vproj_done", s))
                                    else:
                                        k.wait(t, ("expp", s, pu, pqh, KT // 2 - 1, 0))
                                        k.wait(t, ("expp", s, pu, pqh, KT // 2 - 1, 1))
                                if ki == 0 and pp >= 1:
                                    k.wait(t, ("expp", s, u, qh, pp - 1, 0))
                                    k.wait(t, ("expp", s, u, qh, pp - 1, 1))
                                t.matmul(pA_[:, ki * QW:(ki + 1) * QW],
                                         lhsT=kT[0:64, u, kt * 128:(kt + 1) * 128],
                                         rhs=qT[0:64, u, qh * QW:(qh + 1) * QW],
                                         start=True, stop=True, inc=1)
                                k.ck(("smm", s, u, qh, kt, 0), "pe")
                                t.matmul(pB_[:, ki * QW:(ki + 1) * QW],
                                         lhsT=kT[64:128, u, kt * 128:(kt + 1) * 128],
                                         rhs=qT[64:128, u, qh * QW:(qh + 1) * QW],
                                         start=True, stop=True, inc=1)
                                k.ck(("smm", s, u, qh, kt, 1), "pe")
                            for hb in range(2):
                                k.wait(s_, ("smm", s, u, qh, 2 * pp + 1, hb))
                                if pp >= 2:
                                    k.wait(s_, ("avmm", s, u, qh, 2 * pp - 3, hb))
                                s_.activation(out=abuf[:, pp % 2, hb, :],
                                              in_=pA_[:, :] if hb == 0 else pB_[:, :],
                                              func=AF.Exp, bias=negM[:], scale=1.0)
                                k.ck(("expp", s, u, qh, pp, hb), "sc")
                        if pp >= 1:
                            pa_i = pp - 1
                            for ki in range(2):
                                ka = 2 * pa_i + ki
                                for hb in range(2):
                                    if ki == 0:
                                        k.wait(t, ("expp", s, u, qh, pa_i, hb))
                                    if ka == 0 and len(hist_oT) >= 2:
                                        k.wait(t, ("otev", s) + hist_oT[-2] + (hb,))
                                    pO = (pOA if hb == 0 else pOB)[qpar]
                                    t.matmul(pO[0:65, :],
                                             lhsT=vext[:, ka, 2 * u + hb, :],
                                             rhs=abuf[:, pa_i % 2, hb, ki * QW:(ki + 1) * QW],
                                             start=(ka == 0), stop=(ka == KT - 1), inc=1)
                                    k.ck(("avmm", s, u, qh, ka, hb), "pe")
                    for hb in range(2):
                        pO = (pOA if hb == 0 else pOB)[qpar]
                        k.wait(v, ("avmm", s, u, qh, KT - 1, hb))
                        v.reciprocal(out=rbs[0:1, :], in_=pO[64:65, :])
                        v.tensor_copy(out=rrowb[0:1, :], in_=rbs[0:1, :])
                        k.ck(("rcp", s, u, qh, hb), "dv")
                        k.wait(t, ("rcp", s, u, qh, hb))
                        t.matmul(pO[64:128, :], lhsT=ones_b1[0:1, 0:64],
                                 rhs=rrowb[0:1, :], start=True, stop=True, inc=1)
                        k.ck(("rbc", s, u, qh, hb), "pe")
                        k.wait(v, ("rbc", s, u, qh, hb))
                        v.tensor_copy(out=rbs[0:64, :], in_=pO[64:128, :])
                        k.ck(("rbsv", s, u, qh, hb), "dv")
                        v.tensor_tensor(out=oT[hb * 64:hb * 64 + 64, u, qh * QW:(qh + 1) * QW],
                                        in0=pO[0:64, :], in1=rbs[0:64, :], op=ALU.mult)
                        k.ck(("otev", s, u, qh, hb), "dv")
                    first = False
                    pu, pqh = u, qh
                    hist_oT.append((u, qh))
            k.ck(("oT_done", s), "dv")
            k.ck(("av_done", s), "pe")

            if stop_after == "attn":
                k.wait(g, ("oT_done", s))
                g.dma(out_d[s * P:(s + 1) * P, :].rearrange("p (t d) -> p t d", d=D), residv, "do")
                k.ck(("outd", s), "do")
                return
            for tt in range(TT):
                par = tt % 2
                for half in range(2):
                    if tt == 0:
                        k.wait(t, ("wld", s, 3, half))
                        if half == 0:
                            k.wait(t, ("oT_done", s))
                            k.wait(t, "smload")
                            k.wait(t, "smload_b")
                    if tt >= 2 and half == 0:
                        k.wait(t, ("aoev", s, tt - 2))
                    for jc in range(8):
                        t.matmul(pAO[par][:, half * 512:(half + 1) * 512],
                                 lhsT=oT[:, jc, tt * 128:(tt + 1) * 128],
                                 rhs=wsl[half][:, jc, :],
                                 start=(jc == 0), stop=False, inc=0)
                    t.matmul(pAO[par][:, half * 512:(half + 1) * 512],
                             lhsT=ones_b1[0:1, :],
                             rhs=bffr[0:1, half * 512:(half + 1) * 512],
                             start=False, stop=True,
                             inc=(1 if half == 1 else 0))
                k.ck(("aomm", s, tt), "pe")
                k.wait(v, ("aomm", s, tt))
                v.tensor_tensor(out=residv[:, tt, :], in0=pAO[par][:, :],
                                in1=residv[:, tt, :], op=ALU.add)
                k.ck(("aoev", s, tt), "dv")
            k.ck(("ao_done", s), "dv")

            if stop_after == "ao":
                k.wait(g, ("ao_done", s))
                g.dma(out_d[s * P:(s + 1) * P, :].rearrange("p (t d) -> p t d", d=D), residv, "do")
                k.ck(("outd", s), "do")
                return
            for tt in range(TT):
                xt = residv[:, tt, :]
                k.wait(s_, ("aoev", s, tt))
                s_.activation(out=pdum2[:, :], in_=xt, func=AF.Copy, scale=1.0 / D,
                              accum_out=st("mean", tt))
                s_.activation(out=pdum2[:, :], in_=xt, func=AF.Square, scale=1.0 / 32.0,
                              accum_out=st("ss", tt))
                k.ck(("l2st", s, tt), "sc")
                if stop_after == "l2st":
                    continue
                k.wait(v, ("l2st", s, tt))
                v.tensor_scalar(out=st("msq", tt), in0=st("mean", tt),
                                scalar1=st("mean", tt), scalar2=None, op0=ALU.mult)
                v.tensor_tensor(out=st("var", tt), in0=st("ss", tt), in1=st("msq", tt),
                                op=ALU.subtract)
                k.ck(("l2var", s, tt), "dv")
                k.wait(s_, ("l2var", s, tt))
                s_.activation(out=st("lnv", tt), in_=st("var", tt), func=AF.Ln,
                              bias=epsb[:], scale=1.0)
                s_.activation(out=st("rstd", tt), in_=st("lnv", tt), func=AF.Exp, scale=-0.5)
                k.ck(("l2rstd", s, tt), "sc")
                k.wait(v, ("l2rstd", s, tt))
                v.tensor_scalar(out=st("negmz", tt), in0=st("mean", tt),
                                scalar1=st("rstd", tt), scalar2=-1.0,
                                op0=ALU.mult, op1=ALU.mult)
                if tt >= 1:
                    k.wait(v, ("l2tr", s, tt - 1))
                v.tensor_scalar(out=xln[:], in0=xt, scalar1=st("rstd", tt),
                                scalar2=st("negmz", tt), op0=ALU.mult, op1=ALU.add)
                k.ck(("l2h", s, tt), "dv")
                if stop_after == "l2h":
                    continue
                k.wait(t, ("l2h", s, tt))
                if tt == 0:
                    k.wait(t, ("ao_done", s))
                if tt >= 2 and stop_after != "l2trx":
                    k.wait(t, ("l2trev", s, tt - 2))
                    k.wait(t, ("l2trev_f", s, tt - 2))
                for jc in range(8):
                    t.transpose(pxT[tt % 2][:, jc * 128:(jc + 1) * 128],
                                xln[:, jc * 128:(jc + 1) * 128], identf[:],
                                inc=(1 if jc == 7 else 0))
                k.ck(("l2tr", s, tt), "pe")
                if stop_after == "l2trx":
                    continue
                k.wait(v, ("l2tr", s, tt))
                if tt >= 1 and stop_after != "ln2tr":
                    k.wait(v, ("rmm", s, tt - 1))
                v.tensor_copy(out=xlnT[:], in_=pxT[tt % 2][:, :])
                k.ck(("l2trev_f", s, tt), "dv")
                k.wait(s_, ("l2tr", s, tt))
                k.wait(s_, ("l2trev_f", s, tt))
                s_.activation(out=x2T[:, :, tt * 128:(tt + 1) * 128],
                              in_=pxT[tt % 2][:, :].rearrange("p (c q) -> p c q", q=128),
                              func=AF.Copy, scale=1.0)
                k.ck(("l2trev", s, tt), "sc")
                if stop_after == "ln2tr":
                    continue
                k.wait(t, ("l2trev_f", s, tt))
                if tt == 0:
                    k.wait(t, ("oT_done", s))
                if tt >= 2 and stop_after not in ("rmm", "rsum"):
                    k.wait(t, ("rmsk", s, tt - 2))
                for jc in range(8):
                    t.matmul(pL[tt % 2][:, :], lhsT=xlnT[:, jc * 128:(jc + 1) * 128],
                             rhs=rwv[:, jc, :], start=(jc == 0), stop=False, inc=0)
                t.matmul(pL[tt % 2][:, :], lhsT=ones_f1[0:1, :], rhs=rbr[0:1, :],
                         start=False, stop=True, inc=1)
                k.ck(("rmm", s, tt), "pe")
                if stop_after == "rmm":
                    continue
                k.wait(v, ("rmm", s, tt))
                v.tensor_reduce(out=st("maxl", tt), in_=pL[tt % 2][:, :], op=ALU.max,
                                axis=AX.X)
                v.tensor_scalar(out=st("negml", tt), in0=st("maxl", tt), scalar1=-1.0,
                                scalar2=None, op0=ALU.mult)
                k.ck(("rmax", s, tt), "dv")
                k.wait(s_, ("rmax", s, tt))
                s_.activation(out=elb[:], in_=pL[tt % 2][:, :], func=AF.Exp,
                              bias=st("negml", tt), scale=1.0, accum_out=st("sumr", tt))
                k.ck(("rsum", s, tt), "sc")
                if stop_after == "rsum":
                    continue
                k.wait(v, ("rsum", s, tt))
                v.reciprocal(out=st("mp", tt), in_=st("sumr", tt))
                v.tensor_scalar(out=msk[:], in0=pL[tt % 2][:, :], scalar1=st("maxl", tt),
                                scalar2=None, op0=ALU.is_equal)
                v.tensor_scalar(out=sctab[:, tt, :], in0=msk[:], scalar1=st("mp", tt),
                                scalar2=None, op0=ALU.mult)
                k.ck(("rmsk", s, tt), "dv")
            k.ck(("router_done", s), "dv")

            if stop_after in ("router", "ln2tr", "rmm", "rsum", "l2st", "l2h", "l2trx"):
                k.wait(g, ("router_done", s))
                g.dma(out_d[s * P:(s + 1) * P, :].rearrange("p (t d) -> p t d", d=D), residv, "do")
                k.ck(("outd", s), "do")
                return
            for gidx0 in range(2):
                ei, gi = g_list[gidx0]
                k.wait(g, ("ao_done", s))
                g.dma(wsl[gidx0 % 2],
                      w1_in[(ei * 16 + gi) * P:(ei * 16 + gi + 1) * P, :]
                      .rearrange("p (c f) -> p c f", f=512),
                      "dwa" if gidx0 % 2 == 0 else "dwb")
                k.ck(("w1ld", s, gidx0), "dwa" if gidx0 % 2 == 0 else "dwb")
            for zidx0 in range(2):
                ei, qi = z_list[zidx0]
                k.wait(g, ("router_done", s))
                g.dma(w2sl[zidx0 % 2],
                      w2_in[(ei * 4 + qi) * P:(ei * 4 + qi + 1) * P, :]
                      .rearrange("p (c f) -> p c f", f=256),
                      "dza" if zidx0 % 2 == 0 else "dzb")
                k.ck(("w2ld", s, zidx0), "dza" if zidx0 % 2 == 0 else "dzb")

            gidx = 0
            zidx = 0
            pprev = None
            py_hist = []
            for m in range(NM):
                for ei in range(NE):
                    for gi in range(16):
                        slot = gidx % 2
                        for i2 in range(2):
                            i = 2 * gi + i2
                            if i2 == 0:
                                k.wait(t, ("w1ld", s, gidx))
                            if gidx == 0 and i2 == 0 and m == 0:
                                k.wait(t, ("router_done", s))
                                k.wait(t, ("l2trev", s, TT - 1))
                            if pprev is not None:
                                k.wait(t, ("silu", s, *pprev))
                            nw = min(512, MW)
                            for n in range(0, MW, nw):
                                for jc in range(8):
                                    t.matmul(pa[:, n:n + nw],
                                             lhsT=wsl[slot][:, jc, i2 * 128:(i2 + 1) * 128],
                                             rhs=x2T[:, jc, m * MW + n:m * MW + n + nw],
                                             start=(jc == 0), stop=(jc == 7),
                                             inc=(1 if (jc == 7 and n + nw == MW) else 0))
                            k.ck(("p1a", s, m, ei, i), "pe")
                            if pprev is not None:
                                k.wait(t, ("hmul", s, *pprev))
                            for n in range(0, MW, nw):
                                for jc in range(8):
                                    t.matmul(pg_[:, n:n + nw],
                                             lhsT=wsl[slot][:, jc, 256 + i2 * 128:256 + (i2 + 1) * 128],
                                             rhs=x2T[:, jc, m * MW + n:m * MW + n + nw],
                                             start=(jc == 0), stop=(jc == 7),
                                             inc=(1 if (jc == 7 and n + nw == MW) else 0))
                            k.ck(("p1g", s, m, ei, i), "pe")
                            if i2 == 1:
                                k.ck(("p1cons", s, gidx), "pe")
                            k.wait(s_, ("p1a", s, m, ei, i))
                            if pprev is not None:
                                k.wait(s_, ("hmul", s, *pprev))
                            s_.activation(out=hslot[:, 0, 0:MW], in_=pa[:, :], func=AF.Silu,
                                          scale=1.0)
                            k.ck(("silu", s, m, ei, i), "sc")
                            k.wait(v, ("silu", s, m, ei, i))
                            k.wait(v, ("p1g", s, m, ei, i))
                            v.tensor_tensor(out=hh[:, i, :], in0=pg_[:, :],
                                            in1=hslot[:, 0, 0:MW], op=ALU.mult)
                            k.ck(("hmul", s, m, ei, i), "dv")
                            pprev = (m, ei, i)
                        if gidx + 2 < NG1:
                            nei, ngi = g_list[gidx + 2]
                            k.wait(g, ("p1cons", s, gidx))
                            g.dma(wsl[gidx % 2],
                                  w1_in[(nei * 16 + ngi) * P:(nei * 16 + ngi + 1) * P, :]
                                  .rearrange("p (c f) -> p c f", f=512),
                                  "dwa" if gidx % 2 == 0 else "dwb")
                            k.ck(("w1ld", s, gidx + 2), "dwa" if gidx % 2 == 0 else "dwb")
                        gidx += 1
                    for qi in range(4):
                        slot = zidx % 2
                        for tc in range(MTC):
                            par = tc % 2
                            gt = m * MTC + tc
                            if tc == 0:
                                k.wait(t, ("w2ld", s, zidx))
                            if qi == 0 and tc == 0:
                                k.wait(t, ("hmul", s, m, ei, 31))
                                if m == 0 and ei == 0:
                                    k.wait(t, ("rmsk", s, TT - 1))
                            if len(py_hist) >= 2:
                                k.wait(t, ("p2ev", s, *py_hist[-2]))
                            for i in range(32):
                                t.matmul(py[par][:, 0:256],
                                         lhsT=hh[:, i, tc * 128:(tc + 1) * 128],
                                         rhs=w2sl[slot][:, i, :],
                                         start=(i == 0), stop=(i == 31),
                                         inc=(1 if i == 31 else 0))
                            k.ck(("p2mm", s, m, ei, qi, tc), "pe")
                            if tc == MTC - 1:
                                k.ck(("p2cons", s, zidx), "pe")
                            k.wait(v, ("p2mm", s, m, ei, qi, tc))
                            v.scalar_tensor_tensor(
                                out=residv[:, gt, qi * 256:(qi + 1) * 256],
                                in0=py[par][:, 0:256],
                                scalar=sctab[:, gt, ei:ei + 1],
                                in1=residv[:, gt, qi * 256:(qi + 1) * 256],
                                op0=ALU.mult, op1=ALU.add)
                            k.ck(("p2ev", s, m, ei, qi, tc), "dv")
                            py_hist.append((m, ei, qi, tc))
                        if zidx + 2 < NZ2:
                            nei, nqi = z_list[zidx + 2]
                            k.wait(g, ("p2cons", s, zidx))
                            g.dma(w2sl[zidx % 2],
                                  w2_in[(nei * 4 + nqi) * P:(nei * 4 + nqi + 1) * P, :]
                                  .rearrange("p (c f) -> p c f", f=256),
                                  "dza" if zidx % 2 == 0 else "dzb")
                            k.ck(("w2ld", s, zidx + 2), "dza" if zidx % 2 == 0 else "dzb")
                        zidx += 1
                    if ei == NE - 1:
                        k.ck(("m_done", s, m), "dv")
                        k.wait(g, ("m_done", s, m))
                        g.dma(out_d[s * P:(s + 1) * P, :].rearrange("p (t d) -> p t d", d=D)
                              [:, m * MTC:(m + 1) * MTC, :],
                              residv[:, m * MTC:(m + 1) * MTC, :], "do")
                        k.ck(("outd", s, m), "do")
            k.ck(("ffn_done", s), "dv")

        ENGS = [("t", "pe", False), ("v", "dv", True), ("s_", "sc", True),
                ("g", "gp", True), ("sy", "sp", False)]

        kd = K(dry=True)
        margs = {n: E(kd, None, sm, ch) for n, sm, ch in ENGS}
        emit(kd, margs["t"], margs["v"], margs["s_"], margs["g"], margs["sy"])
        chk = kd.chk
        dry_cnt = dict(kd.cnt)

        def real_pass(name):
            def body(eng):
                k = K(dry=False, sems=sems, chk=chk)
                args = {n: E(k, None, sm, ch) for n, sm, ch in ENGS}
                for n, sm, ch in ENGS:
                    if n == name:
                        args[n] = E(k, eng, sm, ch)
                emit(k, args["t"], args["v"], args["s_"], args["g"], args["sy"])
                for snm, val in dry_cnt.items():
                    assert k.cnt[snm] == val, (name, snm, k.cnt[snm], val)
            return body

        block.tensor(real_pass("t"))
        block.vector(real_pass("v"))
        block.scalar(real_pass("s_"))
        block.gpsimd(real_pass("g"))
        block.sync(real_pass("sy"))

    return nc


_NC_CACHE = {}


def _relayout(L0_attn_Wq, L0_attn_Wk, L0_attn_Wv, L0_attn_Wff, L0_attn_bff,
              L0_ffn_router_w, L0_ffn_router_b, L0_ffn_w1, L0_ffn_w2):
    c = np.ascontiguousarray
    f32 = np.float32

    def wrel(W):
        return c(np.asarray(W, f32).T.reshape(8, 128, D).transpose(1, 0, 2)
                 .reshape(128, 8, 2, 512).transpose(0, 2, 1, 3)).reshape(128, 8192)

    return dict(
        wq=wrel(L0_attn_Wq), wk=wrel(L0_attn_Wk), wv=wrel(L0_attn_Wv),
        wf=wrel(L0_attn_Wff),
        bff=c(np.asarray(L0_attn_bff, f32)).reshape(1, D),
        rw=c(np.asarray(L0_ffn_router_w, f32).reshape(8, 128, NE)
             .transpose(1, 0, 2)).reshape(128, 8 * NE),
        rb=c(np.asarray(L0_ffn_router_b, f32)).reshape(1, NE),
        w1=c(np.asarray(L0_ffn_w1, f32).reshape(NE, 8, 128, 2, 16, 256)
             .transpose(0, 4, 2, 1, 3, 5)).reshape(NE * 16 * 128, 8 * 512),
        w2=c(np.asarray(L0_ffn_w2, f32).reshape(NE, 32, 128, 4, 256)
             .transpose(0, 3, 2, 1, 4)).reshape(NE * 4 * 128, 32 * 256),
    )


def kernel(L0_attn_x, L0_attn_Wq, L0_attn_Wk, L0_attn_Wv, L0_attn_Wff,
           L0_attn_bff, L0_ffn_router_w, L0_ffn_router_b, L0_ffn_w1,
           L0_ffn_w2):
    x = np.ascontiguousarray(np.asarray(L0_attn_x, dtype=np.float32))
    B, S, D_ = x.shape
    nseq = B // NCORES
    TT = S // 128
    key = (S, nseq)
    if key not in _NC_CACHE:
        _NC_CACHE[key] = build_program(S=S, NSEQ=nseq)
    nc = _NC_CACHE[key]

    wmap = _relayout(L0_attn_Wq, L0_attn_Wk, L0_attn_Wv, L0_attn_Wff,
                     L0_attn_bff, L0_ffn_router_w, L0_ffn_router_b,
                     L0_ffn_w1, L0_ffn_w2)

    in_maps = []
    for cc in range(NCORES):
        xs = np.ascontiguousarray(
            x[cc * nseq:(cc + 1) * nseq].reshape(nseq, TT, 128, D_)
            .transpose(0, 2, 1, 3)).reshape(nseq * 128, TT * D_)
        mp = dict(wmap)
        mp["x"] = xs
        in_maps.append(mp)

    res = run_bass_kernel_spmd(nc, in_maps, core_ids=list(range(NCORES)))
    out = np.empty((B, S, D_), np.float32)
    for cc in range(NCORES):
        o = res.results[cc]["out"].reshape(nseq, 128, TT, D_).transpose(0, 2, 1, 3)
        out[cc * nseq:(cc + 1) * nseq] = o.reshape(nseq, S, D_)
    return out
